# version 01
# baseline (speedup 1.0000x reference)
import numpy as np
import concourse.bass as bass
import concourse.mybir as mybir
from concourse.bass_utils import run_bass_kernel_spmd
from concourse.alu_op_type import AluOpType

F32 = mybir.dt.float32
BF16 = mybir.dt.bfloat16
AF = mybir.ActivationFunctionType

D_MODEL = 1024
D_INNER = 2048
D_STATE = 16
DT_RANK = 64
D_CONV = 4
NCH_M = D_MODEL // 128
NCH_I = D_INNER // 128
EPS = 1e-5


class Sem:
    __slots__ = ("h", "v")
    def __init__(self, h):
        self.h = h
        self.v = 0


def build_program(T, TB, debug=False):
    NB = T // TB
    assert TB == 512
    nc = bass.Bass("TRN2", target_bir_lowering=False)

    dr = lambda n, sh, dt=F32: nc.dram_tensor(n, sh, dt, kind="ExternalInput").ap()
    xT = dr("xT", [D_MODEL, T])
    W_inT = dr("W_inT", [D_MODEL, 2 * D_INNER])
    conv_w = dr("conv_wc", [128, NCH_I * D_CONV])
    conv_b = dr("conv_bc", [128, NCH_I])
    A_log = dr("A_log_r", [1, D_STATE])
    D_in = dr("D_c", [128, NCH_I])
    W_xT = dr("W_xT", [D_INNER, 96])
    W_dtT = dr("W_dtT", [DT_RANK, D_INNER])
    b_dt = dr("b_dtc", [128, NCH_I])
    W_outT = dr("W_outT", [D_INNER, D_MODEL])
    W_rT = dr("W_rT", [D_MODEL, D_MODEL])
    W_kT = dr("W_kT", [D_MODEL, D_MODEL])
    W_vT = dr("W_vT", [D_MODEL, D_MODEL])
    mu_r = dr("mu_rc", [128, NCH_M])
    mu_k = dr("mu_kc", [128, NCH_M])
    eye_in = dr("eye_in", [128, 128])

    out = nc.dram_tensor("out", [D_MODEL, T], F32, kind="ExternalOutput").ap()
    dbg = {}
    if debug:
        for nm, rows_ in (("d_ln", D_MODEL), ("d_xs", D_INNER), ("d_dt", D_INNER),
                          ("d_gy", D_INNER), ("d_x1", D_MODEL), ("d_ln2", D_MODEL),
                          ("d_mrep", 128), ("d_rrep", 128)):
            dbg[nm] = nc.dram_tensor(nm, [rows_, T], F32, kind="ExternalOutput").ap()
        dbg["d_stc"] = nc.dram_tensor("d_stc", [128, NCH_I * D_STATE], F32,
                                      kind="ExternalOutput").ap()

    scr = lambda n, sh, dt=F32: nc.dram_tensor(n, sh, dt, kind="Internal").ap()
    W_inT_bf = scr("W_inT_bf", [D_MODEL, 2 * D_INNER], BF16)
    W_outT_bf = scr("W_outT_bf", [D_INNER, D_MODEL], BF16)
    W_rT_bf = scr("W_rT_bf", [D_MODEL, D_MODEL], BF16)
    W_kT_bf = scr("W_kT_bf", [D_MODEL, D_MODEL], BF16)
    W_vT_bf = scr("W_vT_bf", [D_MODEL, D_MODEL], BF16)
    bounce = [scr(f"bounce{p}", [4, TB]) for p in range(2)]
    bbc = [scr(f"bbc{p}", [32, TB]) for p in range(2)]
    a_bounce = scr("a_bounce", [1, D_STATE])

    from contextlib import ExitStack
    _st = ExitStack()
    with _st:
        xt = _st.enter_context(nc.sbuf_tensor("xt", [128, NCH_M * TB], F32))
        xsq = _st.enter_context(nc.sbuf_tensor("xsq", [128, TB], F32))
        lnt = _st.enter_context(nc.sbuf_tensor("lnt", [128, NCH_M * TB], BF16))
        rows = _st.enter_context(nc.sbuf_tensor("rows", [1, 3 * TB], F32))
        mrep = _st.enter_context(nc.sbuf_tensor("mrep", [128, TB], F32))
        rrep = _st.enter_context(nc.sbuf_tensor("rrep", [128, TB], F32))
        xrt = _st.enter_context(nc.sbuf_tensor("xrt", [128, NCH_I * (TB + 3)], BF16))
        halo = _st.enter_context(nc.sbuf_tensor("halo", [128, NCH_I * 3], BF16))
        convacc = _st.enter_context(nc.sbuf_tensor("convacc", [128, 2 * TB], F32))
        szt = _st.enter_context(nc.sbuf_tensor("szt", [128, NCH_I * TB], BF16))
        dtt = _st.enter_context(nc.sbuf_tensor("dtt", [128, NCH_I * TB], BF16))
        xdbl = _st.enter_context(nc.sbuf_tensor("xdbl", [96, TB], F32))
        dtr = _st.enter_context(nc.sbuf_tensor("dtr", [64, TB], BF16))
        brep = _st.enter_context(nc.sbuf_tensor("brep", [128, D_STATE * TB], BF16))
        crep = _st.enter_context(nc.sbuf_tensor("crep", [128, D_STATE * TB], BF16))
        vch = _st.enter_context(nc.sbuf_tensor("vch", [128, TB], BF16))
        abuf = _st.enter_context(nc.sbuf_tensor("abuf", [128, 2 * TB], F32))
        ubuf = _st.enter_context(nc.sbuf_tensor("ubuf", [128, TB], BF16))
        hbuf = _st.enter_context(nc.sbuf_tensor("hbuf", [128, 2 * TB], F32))
        chbuf = _st.enter_context(nc.sbuf_tensor("chbuf", [128, 2 * TB], BF16))
        ytmp = _st.enter_context(nc.sbuf_tensor("ytmp", [128, TB], F32))
        stc = _st.enter_context(nc.sbuf_tensor("stc", [128, NCH_I * D_STATE], F32))
        gat = _st.enter_context(nc.sbuf_tensor("gat", [128, NCH_I * TB], BF16))
        x1t = _st.enter_context(nc.sbuf_tensor("x1t", [128, NCH_M * TB], F32))
        ln2p = _st.enter_context(nc.sbuf_tensor("ln2p", [128, TB], BF16))
        carry = _st.enter_context(nc.sbuf_tensor("carry", [128, NCH_M], BF16))
        ffntmp = _st.enter_context(nc.sbuf_tensor("ffntmp", [128, TB], BF16))
        x2t = _st.enter_context(nc.sbuf_tensor("x2t", [128, TB], F32))
        wslab = _st.enter_context(nc.sbuf_tensor("wslab", [128, 16 * 128], BF16))
        wsx = _st.enter_context(nc.sbuf_tensor("wsx", [128, NCH_I * 96], BF16))
        wsdt = _st.enter_context(nc.sbuf_tensor("wsdt", [64, D_INNER], BF16))
        cwc = _st.enter_context(nc.sbuf_tensor("cwc", [128, NCH_I * D_CONV], F32))
        cbc = _st.enter_context(nc.sbuf_tensor("cbc", [128, NCH_I], F32))
        bdtc = _st.enter_context(nc.sbuf_tensor("bdtc", [128, NCH_I], F32))
        dc = _st.enter_context(nc.sbuf_tensor("dc", [128, NCH_I], F32))
        murc = _st.enter_context(nc.sbuf_tensor("murc", [128, NCH_M], F32))
        mukc = _st.enter_context(nc.sbuf_tensor("mukc", [128, NCH_M], F32))
        omrc = _st.enter_context(nc.sbuf_tensor("omrc", [128, NCH_M], F32))
        omkc = _st.enter_context(nc.sbuf_tensor("omkc", [128, NCH_M], F32))
        arow = _st.enter_context(nc.sbuf_tensor("arow", [1, D_STATE], F32))
        abc = _st.enter_context(nc.sbuf_tensor("abc", [128, D_STATE], F32))
        eyeb = _st.enter_context(nc.sbuf_tensor("eyeb", [128, 128], BF16))
        eyef = _st.enter_context(nc.sbuf_tensor("eyef", [128, 128], F32))
        onesf = _st.enter_context(nc.sbuf_tensor("onesf", [128, 1], F32))
        ln2t = lnt
        xft = szt.ap()[:, 0:NCH_M * TB]
        xkt = szt.ap()[:, NCH_M * TB:2 * NCH_M * TB]
        rt = dtt.ap()[:, 0:NCH_M * TB]
        kat = dtt.ap()[:, NCH_M * TB:2 * NCH_M * TB]
        ps_a = _st.enter_context(nc.psum_tensor("ps_a", [128, 512], F32))
        ps_b = _st.enter_context(nc.psum_tensor("ps_b", [128, 512], F32))
        ps_c = _st.enter_context(nc.psum_tensor("ps_c", [128, 512], F32))
        ps_d = _st.enter_context(nc.psum_tensor("ps_d", [128, 512], F32))
        ps_y0 = _st.enter_context(nc.psum_tensor("ps_y0", [128, 512], F32))
        ps_y1 = _st.enter_context(nc.psum_tensor("ps_y1", [128, 512], F32))
        ps_s1 = _st.enter_context(nc.psum_tensor("ps_s1", [1, 512], F32))
        ps_s2 = _st.enter_context(nc.psum_tensor("ps_s2", [1, 512], F32))
        _q_w = _st.enter_context(nc.semaphore("q_w"))
        _q_x = _st.enter_context(nc.semaphore("q_x"))
        _q_g = _st.enter_context(nc.semaphore("q_g"))
        _q_o = _st.enter_context(nc.semaphore("q_o"))
        _s_T = _st.enter_context(nc.semaphore("s_T"))
        _s_V = _st.enter_context(nc.semaphore("s_V"))
        _s_S = _st.enter_context(nc.semaphore("s_S"))
        block = _st.enter_context(nc.Block())
        q_w, q_x, q_g, q_o = Sem(_q_w), Sem(_q_x), Sem(_q_g), Sem(_q_o)
        s_T, s_V, s_S = Sem(_s_T), Sem(_s_V), Sem(_s_S)
        class _Pending:
            __slots__ = ("incs",)
            def __init__(self):
                self.incs = []
            def then_inc(self, sem, n):
                self.incs.append((sem, n))
                return self

        class _Rec:
            def __init__(self):
                self.ops = []
            def __getattr__(self, meth):
                def call(*a, **kw):
                    p = _Pending()
                    self.ops.append((meth, a, kw, p))
                    return p
                return call
            def replay(self, eng):
                for meth, a, kw, p in self.ops:
                    instr = getattr(eng, meth)(*a, **kw)
                    for sem, n in p.incs:
                        instr.then_inc(sem, n)

        sy, g, t, v, s = _Rec(), _Rec(), _Rec(), _Rec(), _Rec()
        PS = [ps_a, ps_b, ps_c, ps_d]

        def dma(eng, q, **kw):
            eng.dma_start(**kw).then_inc(q.h, 16)
            q.v += 16

        def vi(instr):
            instr.then_inc(s_V.h, 1)
            s_V.v += 1

        def si(instr):
            instr.then_inc(s_S.h, 1)
            s_S.v += 1

        def ti(instr):
            instr.then_inc(s_T.h, 1)
            s_T.v += 1

        dma(sy, q_x, out=cwc[:], in_=conv_w[:])
        dma(sy, q_x, out=cbc[:], in_=conv_b[:])
        dma(sy, q_x, out=bdtc[:], in_=b_dt[:])
        dma(sy, q_x, out=dc[:], in_=D_in[:])
        dma(sy, q_x, out=murc[:], in_=mu_r[:])
        dma(sy, q_x, out=mukc[:], in_=mu_k[:])
        dma(sy, q_x, out=arow[:], in_=A_log[:])
        dma(sy, q_x, out=eyef[:], in_=eye_in[:])
        dma(g, q_g, out=wsdt[:], in_=W_dtT[:])
        for c in range(NCH_I):
            dma(g, q_g, out=wsx[:, c * 96:(c + 1) * 96],
                in_=W_xT[c * 128:(c + 1) * 128, :])

        s.wait_ge(q_x.h, q_x.v)
        si(s.activation(out=arow[:], in_=arow[:], func=AF.Exp))
        v.wait_ge(s_S.h, s_S.v)
        vi(v.tensor_scalar(arow[:], arow[:], -1.0, None, AluOpType.mult))
        g.wait_ge(s_V.h, s_V.v)
        dma(g, q_g, out=a_bounce[:], in_=arow[:])
        g.wait_ge(q_g.h, q_g.v)
        dma(g, q_g, out=abc[:], in_=a_bounce.broadcast_to((128, D_STATE)))

        v.wait_ge(q_x.h, q_x.v)
        vi(v.tensor_copy(eyeb[:], eyef[:]))
        vi(v.memset(onesf[:], 1.0))
        vi(v.memset(stc[:], 0.0))
        vi(v.memset(halo[:], 0.0))
        vi(v.memset(carry[:], 0.0))
        vi(v.tensor_scalar(omrc[:], murc[:], -1.0, 1.0, AluOpType.mult, AluOpType.add))
        vi(v.tensor_scalar(omkc[:], mukc[:], -1.0, 1.0, AluOpType.mult, AluOpType.add))

        def cast_weight(src, dst, R, C):
            for r0 in range(0, R, 128):
                sy.wait_ge(q_g.h, q_g.v)
                dma(sy, q_w, out=xt[:, :C], in_=src[r0:r0 + 128, :])
                v.wait_ge(q_w.h, q_w.v)
                vi(v.tensor_copy(gat[:, :C], xt[:, :C]))
                g.wait_ge(s_V.h, s_V.v)
                dma(g, q_g, out=dst[r0:r0 + 128, :], in_=gat[:, :C])

        cast_weight(W_inT, W_inT_bf, D_MODEL, 2 * D_INNER)
        cast_weight(W_outT, W_outT_bf, D_INNER, D_MODEL)
        cast_weight(W_rT, W_rT_bf, D_MODEL, D_MODEL)
        cast_weight(W_kT, W_kT_bf, D_MODEL, D_MODEL)
        cast_weight(W_vT, W_vT_bf, D_MODEL, D_MODEL)

        SLABW = 16 * 128

        def load_wslab(src_bf, R, m0):
            nk = R // 128
            sy.wait_ge(s_T.h, s_T.v)
            for k in range(nk):
                dma(sy, q_w,
                    out=wslab[:, k * 128:(k + 1) * 128],
                    in_=src_bf[k * 128:(k + 1) * 128, m0 * 128:(m0 + 1) * 128])

        def ln_stats_apply(src_t, dst_t, par, dump_t0=None):
            nch = NCH_M
            for c in range(nch):
                sl = src_t[:, c * TB:(c + 1) * TB]
                s.wait_ge(s_T.h, s_T.v)
                si(s.activation(out=xsq[:], in_=sl, func=AF.Square))
                t.wait_ge(s_S.h, s_S.v)
                ti(t.matmul(ps_s1[:, :], onesf[:], sl, start=(c == 0), stop=(c == nch - 1)))
                ti(t.matmul(ps_s2[:, :], onesf[:], xsq[:], start=(c == 0), stop=(c == nch - 1)))
            v.wait_ge(s_T.h, s_T.v)
            r_mean = rows[0:1, 0:TB]
            r_q = rows[0:1, TB:2 * TB]
            r_var = rows[0:1, 2 * TB:3 * TB]
            vi(v.tensor_scalar(r_mean, ps_s1[0:1, :], 1.0 / D_MODEL, None, AluOpType.mult))
            vi(v.tensor_scalar(r_q, ps_s2[0:1, :], 1.0 / D_MODEL, None, AluOpType.mult))
            vi(v.tensor_tensor(r_var, r_mean, r_mean, AluOpType.mult))
            vi(v.tensor_tensor(r_var, r_q, r_var, AluOpType.subtract))
            vi(v.tensor_scalar(r_var, r_var, EPS, None, AluOpType.add))
            s.wait_ge(s_V.h, s_V.v)
            si(s.activation(out=r_var, in_=r_var, func=AF.Ln))
            si(s.activation(out=r_var, in_=r_var, func=AF.Exp, scale=-0.5))
            g.wait_ge(s_V.h, s_V.v)
            g.wait_ge(s_S.h, s_S.v)
            dma(g, q_g, out=bounce[par][0:1, :], in_=r_mean)
            dma(g, q_g, out=bounce[par][1:2, :], in_=r_var)
            g.wait_ge(q_g.h, q_g.v)
            dma(g, q_g, out=mrep[:], in_=bounce[par][0:1, :].broadcast_to((128, TB)))
            dma(g, q_g, out=rrep[:], in_=bounce[par][1:2, :].broadcast_to((128, TB)))
            v.wait_ge(q_g.h, q_g.v)
            for c in range(nch):
                sl = src_t[:, c * TB:(c + 1) * TB]
                dl = dst_t[:, c * TB:(c + 1) * TB]
                vi(v.tensor_tensor(convacc[:, 0:TB], sl, mrep[:], AluOpType.subtract))
                vi(v.tensor_tensor(dl, convacc[:, 0:TB], rrep[:], AluOpType.mult))
            if debug and dump_t0 is not None:
                g.wait_ge(s_V.h, s_V.v)
                dma(g, q_g, out=dbg["d_mrep"][:, dump_t0:dump_t0 + TB], in_=mrep[:])
                dma(g, q_g, out=dbg["d_rrep"][:, dump_t0:dump_t0 + TB], in_=rrep[:])
                v.wait_ge(q_g.h, q_g.v)

        def xs_c(c):
            return xrt[:, c * (TB + 3) + 3:(c + 1) * (TB + 3)]

        for blk in range(NB):
            par = blk & 1
            t0 = blk * TB

            for e in (sy, g, t, v, s):
                e.wait_ge(s_V.h, s_V.v)
                e.wait_ge(s_S.h, s_S.v)
                e.wait_ge(s_T.h, s_T.v)
                e.wait_ge(q_g.h, q_g.v)
                e.wait_ge(q_w.h, q_w.v)
                e.wait_ge(q_x.h, q_x.v)
            for c in range(NCH_M):
                dma(sy, q_x, out=xt[:, c * TB:(c + 1) * TB],
                    in_=xT[c * 128:(c + 1) * 128, t0:t0 + TB])
            s.wait_ge(q_x.h, q_x.v)
            v.wait_ge(q_x.h, q_x.v)
            t.wait_ge(q_x.h, q_x.v)

            ln_stats_apply(xt, lnt, par, dump_t0=t0)

            for m in range(2 * NCH_I):
                load_wslab(W_inT_bf, D_MODEL, m)
                t.wait_ge(q_w.h, q_w.v)
                t.wait_ge(s_V.h, s_V.v)
                t.wait_ge(s_S.h, s_S.v)
                ps = PS[m & 1]
                for k in range(NCH_M):
                    ti(t.matmul(ps[:, :],
                                wslab[:, k * 128:(k + 1) * 128],
                                lnt[:, k * TB:(k + 1) * TB],
                                start=(k == 0), stop=(k == NCH_M - 1)))
                if m < NCH_I:
                    v.wait_ge(s_T.h, s_T.v)
                    vi(v.tensor_copy(xrt[:, m * (TB + 3) + 3:(m + 1) * (TB + 3)], ps[:, :]))
                else:
                    c = m - NCH_I
                    s.wait_ge(s_T.h, s_T.v)
                    si(s.activation(out=szt[:, c * TB:(c + 1) * TB], in_=ps[:, :], func=AF.Silu))

            for c in range(NCH_I):
                xr_ch = xrt[:, c * (TB + 3):(c + 1) * (TB + 3)]
                v.wait_ge(s_S.h, s_S.v)
                vi(v.tensor_copy(xr_ch[:, 0:3], halo[:, c * 3:(c + 1) * 3]))
                acc0 = convacc[:, 0:TB]
                acc1 = convacc[:, TB:2 * TB]
                vi(v.tensor_scalar(acc0, xr_ch[:, 0:TB], cwc[:, c * 4:c * 4 + 1], None, AluOpType.mult))
                vi(v.scalar_tensor_tensor(acc1, xr_ch[:, 1:1 + TB], cwc[:, c * 4 + 1:c * 4 + 2], acc0,
                                          AluOpType.mult, AluOpType.add))
                vi(v.scalar_tensor_tensor(acc0, xr_ch[:, 2:2 + TB], cwc[:, c * 4 + 2:c * 4 + 3], acc1,
                                          AluOpType.mult, AluOpType.add))
                vi(v.scalar_tensor_tensor(acc1, xr_ch[:, 3:3 + TB], cwc[:, c * 4 + 3:c * 4 + 4], acc0,
                                          AluOpType.mult, AluOpType.add))
                vi(v.tensor_copy(halo[:, c * 3:(c + 1) * 3], xr_ch[:, TB:TB + 3]))
                s.wait_ge(s_V.h, s_V.v)
                si(s.activation(out=xr_ch[:, 3:3 + TB], in_=acc1, func=AF.Silu,
                                bias=cbc[:, c:c + 1]))

            t.wait_ge(s_S.h, s_S.v)
            t.wait_ge(s_V.h, s_V.v)
            for k in range(NCH_I):
                ti(t.matmul(ps_a[0:96, :], wsx[:, k * 96:(k + 1) * 96], xs_c(k),
                            start=(k == 0), stop=(k == NCH_I - 1)))
            v.wait_ge(s_T.h, s_T.v)
            vi(v.tensor_copy(xdbl[:, :], ps_a[0:96, :]))
            vi(v.tensor_copy(dtr[:, :], ps_a[0:64, :]))
            g.wait_ge(s_V.h, s_V.v)
            dma(g, q_g, out=bbc[par][:, :], in_=xdbl[64:96, :])
            g.wait_ge(q_g.h, q_g.v)
            for si_ in range(D_STATE):
                dma(g, q_g, out=brep[:, si_ * TB:(si_ + 1) * TB],
                    in_=bbc[par][si_:si_ + 1, :].broadcast_to((128, TB)))
                dma(g, q_g, out=crep[:, si_ * TB:(si_ + 1) * TB],
                    in_=bbc[par][16 + si_:17 + si_, :].broadcast_to((128, TB)))

            t.wait_ge(s_V.h, s_V.v)
            for m in range(NCH_I):
                ps = PS[2 + (m & 1)]
                t.wait_ge(s_S.h, s_S.v)
                ti(t.matmul(ps[:, :], wsdt[:, m * 128:(m + 1) * 128], dtr[:, :],
                            start=True, stop=True))
                s.wait_ge(s_T.h, s_T.v)
                si(s.activation(out=xsq[:], in_=ps[:, :], func=AF.Exp,
                                bias=bdtc[:, m:m + 1]))
                si(s.activation(out=dtt[:, m * TB:(m + 1) * TB], in_=xsq[:],
                                func=AF.Ln, bias=1.0))

            v.wait_ge(q_g.h, q_g.v)
            v.wait_ge(s_S.h, s_S.v)
            for c in range(NCH_I):
                dt_ch = dtt[:, c * TB:(c + 1) * TB]
                vi(v.tensor_tensor(vch[:], dt_ch, xs_c(c), AluOpType.mult))
                psy = [ps_y0, ps_y1][c & 1]
                for si_ in range(D_STATE):
                    ab = abuf[:, (si_ & 1) * TB:((si_ & 1) + 1) * TB]
                    hb = hbuf[:, (si_ & 1) * TB:((si_ & 1) + 1) * TB]
                    chb = chbuf[:, (si_ & 1) * TB:((si_ & 1) + 1) * TB]
                    s.wait_ge(s_V.h, max(0, s_V.v - 4))
                    si(s.activation(out=ab, in_=dt_ch, func=AF.Exp,
                                    scale=abc[:, si_:si_ + 1]))
                    v.wait_ge(s_S.h, s_S.v)
                    vi(v.tensor_tensor(ubuf[:], vch[:], brep[:, si_ * TB:(si_ + 1) * TB],
                                       AluOpType.mult))
                    stcol = stc[:, c * D_STATE + si_: c * D_STATE + si_ + 1]
                    v.wait_ge(s_T.h, max(0, s_T.v - 1))
                    vi(v.tensor_tensor_scan(hb, ab, ubuf[:], stcol,
                                            AluOpType.mult, AluOpType.add))
                    v.wait_ge(s_V.h, s_V.v)
                    vi(v.tensor_scalar(stcol, hb[:, TB - 1:TB], 1.0, None, AluOpType.mult))
                    vi(v.tensor_tensor(chb, hb, crep[:, si_ * TB:(si_ + 1) * TB],
                                       AluOpType.mult))
                    t.wait_ge(s_V.h, s_V.v)
                    ti(t.matmul(psy[:, :], eyeb[:], chb,
                                start=(si_ == 0), stop=(si_ == D_STATE - 1)))
                v.wait_ge(s_T.h, s_T.v)
                vi(v.tensor_scalar(ubuf[:], xs_c(c), dc[:, c:c + 1], None, AluOpType.mult))
                vi(v.tensor_tensor(ytmp[:], psy[:, :], ubuf[:], AluOpType.add))
                vi(v.tensor_tensor(gat[:, c * TB:(c + 1) * TB], ytmp[:],
                                   szt[:, c * TB:(c + 1) * TB], AluOpType.mult))

            if debug:
                g.wait_ge(s_V.h, s_V.v)
                for c in range(NCH_M):
                    dma(g, q_g, out=dbg["d_ln"][c * 128:(c + 1) * 128, t0:t0 + TB],
                        in_=lnt[:, c * TB:(c + 1) * TB])
                for c in range(NCH_I):
                    dma(g, q_g, out=dbg["d_xs"][c * 128:(c + 1) * 128, t0:t0 + TB],
                        in_=xs_c(c))
                    dma(g, q_g, out=dbg["d_dt"][c * 128:(c + 1) * 128, t0:t0 + TB],
                        in_=dtt[:, c * TB:(c + 1) * TB])
                    dma(g, q_g, out=dbg["d_gy"][c * 128:(c + 1) * 128, t0:t0 + TB],
                        in_=gat[:, c * TB:(c + 1) * TB])

            if debug:
                v.wait_ge(q_g.h, q_g.v)
            for m in range(NCH_M):
                load_wslab(W_outT_bf, D_INNER, m)
                t.wait_ge(q_w.h, q_w.v)
                t.wait_ge(s_V.h, s_V.v)
                ps = PS[m & 1]
                for k in range(NCH_I):
                    ti(t.matmul(ps[:, :],
                                wslab[:, k * 128:(k + 1) * 128],
                                gat[:, k * TB:(k + 1) * TB],
                                start=(k == 0), stop=(k == NCH_I - 1)))
                v.wait_ge(s_T.h, s_T.v)
                vi(v.tensor_tensor(x1t[:, m * TB:(m + 1) * TB], ps[:, :],
                                   xt[:, m * TB:(m + 1) * TB], AluOpType.add))

            if debug:
                g.wait_ge(s_V.h, s_V.v)
                for c in range(NCH_M):
                    dma(g, q_g, out=dbg["d_x1"][c * 128:(c + 1) * 128, t0:t0 + TB],
                        in_=x1t[:, c * TB:(c + 1) * TB])

            if debug:
                v.wait_ge(q_g.h, q_g.v)
            ln_stats_apply(x1t, ln2t, par)

            if debug:
                g.wait_ge(s_V.h, s_V.v)
                for c in range(NCH_M):
                    dma(g, q_g, out=dbg["d_ln2"][c * 128:(c + 1) * 128, t0:t0 + TB],
                        in_=ln2t[:, c * TB:(c + 1) * TB])

            if debug:
                v.wait_ge(q_g.h, q_g.v)
            for c in range(NCH_M):
                l2 = ln2t[:, c * TB:(c + 1) * TB]
                vi(v.tensor_copy(ln2p[:, 0:1], carry[:, c:c + 1]))
                vi(v.tensor_copy(ln2p[:, 1:TB], l2[:, 0:TB - 1]))
                vi(v.tensor_copy(carry[:, c:c + 1], l2[:, TB - 1:TB]))
                xf = xft[:, c * TB:(c + 1) * TB]
                xk = xkt[:, c * TB:(c + 1) * TB]
                vi(v.tensor_scalar(xf, l2, murc[:, c:c + 1], None, AluOpType.mult))
                vi(v.scalar_tensor_tensor(ffntmp[:], ln2p[:], omrc[:, c:c + 1], xf,
                                          AluOpType.mult, AluOpType.add))
                vi(v.tensor_copy(xf, ffntmp[:]))
                vi(v.tensor_scalar(xk, l2, mukc[:, c:c + 1], None, AluOpType.mult))
                vi(v.scalar_tensor_tensor(ffntmp[:], ln2p[:], omkc[:, c:c + 1], xk,
                                          AluOpType.mult, AluOpType.add))
                vi(v.tensor_copy(xk, ffntmp[:]))

            for m in range(NCH_M):
                load_wslab(W_rT_bf, D_MODEL, m)
                t.wait_ge(q_w.h, q_w.v)
                t.wait_ge(s_V.h, s_V.v)
                t.wait_ge(s_S.h, s_S.v)
                ps = PS[m & 1]
                for k in range(NCH_M):
                    ti(t.matmul(ps[:, :],
                                wslab[:, k * 128:(k + 1) * 128],
                                xft[:, k * TB:(k + 1) * TB],
                                start=(k == 0), stop=(k == NCH_M - 1)))
                s.wait_ge(s_T.h, s_T.v)
                si(s.activation(out=rt[:, m * TB:(m + 1) * TB], in_=ps[:, :], func=AF.Sigmoid))
            for m in range(NCH_M):
                load_wslab(W_kT_bf, D_MODEL, m)
                t.wait_ge(q_w.h, q_w.v)
                t.wait_ge(s_S.h, s_S.v)
                ps = PS[2 + (m & 1)]
                for k in range(NCH_M):
                    ti(t.matmul(ps[:, :],
                                wslab[:, k * 128:(k + 1) * 128],
                                xkt[:, k * TB:(k + 1) * TB],
                                start=(k == 0), stop=(k == NCH_M - 1)))
                s.wait_ge(s_T.h, s_T.v)
                si(s.activation(out=xsq[:], in_=ps[:, :], func=AF.Relu))
                si(s.activation(out=kat[:, m * TB:(m + 1) * TB], in_=xsq[:], func=AF.Square))

            for m in range(NCH_M):
                load_wslab(W_vT_bf, D_MODEL, m)
                t.wait_ge(q_w.h, q_w.v)
                t.wait_ge(s_S.h, s_S.v)
                t.wait_ge(s_V.h, s_V.v)
                ps = PS[m & 1]
                for k in range(NCH_M):
                    ti(t.matmul(ps[:, :],
                                wslab[:, k * 128:(k + 1) * 128],
                                kat[:, k * TB:(k + 1) * TB],
                                start=(k == 0), stop=(k == NCH_M - 1)))
                v.wait_ge(s_T.h, s_T.v)
                v.wait_ge(q_o.h, q_o.v)
                x2b = x2t[:, :]
                vi(v.tensor_tensor(ffntmp[:], ps[:, :], rt[:, m * TB:(m + 1) * TB],
                                   AluOpType.mult))
                vi(v.tensor_tensor(x2b, ffntmp[:], x1t[:, m * TB:(m + 1) * TB],
                                   AluOpType.add))
                sy.wait_ge(s_V.h, s_V.v)
                dma(sy, q_o, out=out[m * 128:(m + 1) * 128, t0:t0 + TB], in_=x2b)

        if debug:
            g.wait_ge(s_V.h, s_V.v)
            dma(g, q_g, out=dbg["d_stc"][:, :], in_=stc[:, :])
        g.wait_ge(q_o.h, q_o.v)
        if debug:
            g.wait_ge(q_g.h, q_g.v)

        @block.sync
        def _(e):
            sy.replay(e)

        @block.gpsimd
        def _(e):
            g.replay(e)

        @block.tensor
        def _(e):
            t.replay(e)

        @block.vector
        def _(e):
            v.replay(e)

        @block.scalar
        def _(e):
            s.replay(e)

    return nc


_PROG_CACHE = {}


def _get_prog(T, TB, debug=False):
    key = (T, TB, debug)
    if key not in _PROG_CACHE:
        _PROG_CACHE[key] = build_program(T, TB, debug)
    return _PROG_CACHE[key]


def make_in_map(b, T, L0_attn_x, L0_attn_W_in, L0_attn_conv_w, L0_attn_conv_b,
                L0_attn_A_log, L0_attn_D, L0_attn_W_x, L0_attn_W_dt, L0_attn_b_dt,
                L0_attn_W_out, L0_ffn_W_r, L0_ffn_W_k, L0_ffn_W_v, L0_ffn_mu_r,
                L0_ffn_mu_k):
    C = np.ascontiguousarray
    f = np.float32
    return {
        "xT": C(np.asarray(L0_attn_x)[b, :T, :].T.astype(f)),
        "W_inT": C(np.asarray(L0_attn_W_in).T.astype(f)),
        "conv_wc": C(np.asarray(L0_attn_conv_w).reshape(NCH_I, 128, 4)
                     .transpose(1, 0, 2).reshape(128, NCH_I * 4).astype(f)),
        "conv_bc": C(np.asarray(L0_attn_conv_b).reshape(NCH_I, 128).T.astype(f)),
        "A_log_r": C(np.asarray(L0_attn_A_log).reshape(1, D_STATE).astype(f)),
        "D_c": C(np.asarray(L0_attn_D).reshape(NCH_I, 128).T.astype(f)),
        "W_xT": C(np.asarray(L0_attn_W_x).T.astype(f)),
        "W_dtT": C(np.asarray(L0_attn_W_dt).T.astype(f)),
        "b_dtc": C(np.asarray(L0_attn_b_dt).reshape(NCH_I, 128).T.astype(f)),
        "W_outT": C(np.asarray(L0_attn_W_out).T.astype(f)),
        "W_rT": C(np.asarray(L0_ffn_W_r).T.astype(f)),
        "W_kT": C(np.asarray(L0_ffn_W_k).T.astype(f)),
        "W_vT": C(np.asarray(L0_ffn_W_v).T.astype(f)),
        "mu_rc": C(np.asarray(L0_ffn_mu_r).reshape(NCH_M, 128).T.astype(f)),
        "mu_kc": C(np.asarray(L0_ffn_mu_k).reshape(NCH_M, 128).T.astype(f)),
        "eye_in": np.eye(128, dtype=f),
    }


def kernel(**inputs):
    T = 8192
    nc = _get_prog(T, 512)
    in_maps = [make_in_map(c >> 1, T, **inputs) for c in range(8)]
    res = run_bass_kernel_spmd(nc, in_maps, core_ids=list(range(8)))
    B = 4
    outs = []
    for b in range(B):
        o = res.results[2 * b]["out"]
        outs.append(np.ascontiguousarray(o.T)[None])
    return np.concatenate(outs, axis=0)


# version 02
# speedup vs baseline: 1.6466x; 1.6466x over previous
import numpy as np
import concourse.bass as bass
import concourse.mybir as mybir
from concourse.bass_utils import run_bass_kernel_spmd
from concourse.alu_op_type import AluOpType

F32 = mybir.dt.float32
BF16 = mybir.dt.bfloat16
AF = mybir.ActivationFunctionType

D_MODEL = 1024
D_INNER = 2048
D_STATE = 16
DT_RANK = 64
D_CONV = 4
NCH_M = D_MODEL // 128
NCH_I = D_INNER // 128
EPS = 1e-5


class Sem:
    __slots__ = ("h", "v")
    def __init__(self, h):
        self.h = h
        self.v = 0


def build_program(T, TB, debug=False):
    NB = T // TB
    assert TB == 512
    nc = bass.Bass("TRN2", target_bir_lowering=False)

    dr = lambda n, sh, dt=F32: nc.dram_tensor(n, sh, dt, kind="ExternalInput").ap()
    xT = dr("xT", [D_MODEL, T])
    W_inT = dr("W_inT", [D_MODEL, 2 * D_INNER])
    conv_w = dr("conv_wc", [128, NCH_I * D_CONV])
    conv_b = dr("conv_bc", [128, NCH_I])
    A_log = dr("A_log_r", [1, D_STATE])
    D_in = dr("D_c", [128, NCH_I])
    W_xT = dr("W_xT", [D_INNER, 96])
    W_dtT = dr("W_dtT", [DT_RANK, D_INNER])
    b_dt = dr("b_dtc", [128, NCH_I])
    W_outT = dr("W_outT", [D_INNER, D_MODEL])
    W_rT = dr("W_rT", [D_MODEL, D_MODEL])
    W_kT = dr("W_kT", [D_MODEL, D_MODEL])
    W_vT = dr("W_vT", [D_MODEL, D_MODEL])
    mu_r = dr("mu_rc", [128, NCH_M])
    mu_k = dr("mu_kc", [128, NCH_M])
    eye_in = dr("eye_in", [128, 128])

    out = nc.dram_tensor("out", [D_MODEL, T], F32, kind="ExternalOutput").ap()
    dbg = {}
    if debug:
        for nm, rows_ in (("d_ln", D_MODEL), ("d_xs", D_INNER), ("d_dt", D_INNER),
                          ("d_gy", D_INNER), ("d_x1", D_MODEL), ("d_ln2", D_MODEL),
                          ("d_mrep", 128), ("d_rrep", 128)):
            dbg[nm] = nc.dram_tensor(nm, [rows_, T], F32, kind="ExternalOutput").ap()
        dbg["d_stc"] = nc.dram_tensor("d_stc", [128, NCH_I * D_STATE], F32,
                                      kind="ExternalOutput").ap()

    scr = lambda n, sh, dt=F32: nc.dram_tensor(n, sh, dt, kind="Internal").ap()
    W_inT_bf = scr("W_inT_bf", [D_MODEL, 2 * D_INNER], BF16)
    W_outT_bf = scr("W_outT_bf", [D_INNER, D_MODEL], BF16)
    W_rT_bf = scr("W_rT_bf", [D_MODEL, D_MODEL], BF16)
    W_kT_bf = scr("W_kT_bf", [D_MODEL, D_MODEL], BF16)
    W_vT_bf = scr("W_vT_bf", [D_MODEL, D_MODEL], BF16)
    bounce = [scr(f"bounce{p}", [4, TB]) for p in range(2)]
    bbc = [scr(f"bbc{p}", [32, TB]) for p in range(2)]
    a_bounce = scr("a_bounce", [1, D_STATE])

    from contextlib import ExitStack
    _st = ExitStack()
    with _st:
        xt = _st.enter_context(nc.sbuf_tensor("xt", [128, NCH_M * TB], F32))
        xsq = _st.enter_context(nc.sbuf_tensor("xsq", [128, TB], F32))
        lnt = _st.enter_context(nc.sbuf_tensor("lnt", [128, NCH_M * TB], BF16))
        rows = _st.enter_context(nc.sbuf_tensor("rows", [1, 3 * TB], F32))
        mrep = _st.enter_context(nc.sbuf_tensor("mrep", [128, TB], F32))
        rrep = _st.enter_context(nc.sbuf_tensor("rrep", [128, TB], F32))
        xrt = _st.enter_context(nc.sbuf_tensor("xrt", [128, NCH_I * (TB + 3)], BF16))
        halo = _st.enter_context(nc.sbuf_tensor("halo", [128, NCH_I * 3], BF16))
        convacc = _st.enter_context(nc.sbuf_tensor("convacc", [128, 2 * TB], F32))
        szt = _st.enter_context(nc.sbuf_tensor("szt", [128, NCH_I * TB], BF16))
        dtt = _st.enter_context(nc.sbuf_tensor("dtt", [128, NCH_I * TB], BF16))
        xdbl = _st.enter_context(nc.sbuf_tensor("xdbl", [96, TB], F32))
        dtr = _st.enter_context(nc.sbuf_tensor("dtr", [64, TB], BF16))
        brep = _st.enter_context(nc.sbuf_tensor("brep", [128, D_STATE * TB], BF16))
        crep = _st.enter_context(nc.sbuf_tensor("crep", [128, D_STATE * TB], BF16))
        vch = _st.enter_context(nc.sbuf_tensor("vch", [128, TB], BF16))
        abuf = _st.enter_context(nc.sbuf_tensor("abuf", [128, 4 * TB], F32))
        ubuf = _st.enter_context(nc.sbuf_tensor("ubuf", [128, TB], BF16))
        hbuf = _st.enter_context(nc.sbuf_tensor("hbuf", [128, 2 * TB], BF16))
        chbuf = _st.enter_context(nc.sbuf_tensor("chbuf", [128, 2 * TB], BF16))
        ytmp = _st.enter_context(nc.sbuf_tensor("ytmp", [128, TB], F32))
        stc = _st.enter_context(nc.sbuf_tensor("stc", [128, NCH_I * D_STATE], F32))
        gat = _st.enter_context(nc.sbuf_tensor("gat", [128, NCH_I * TB], BF16))
        x1t = _st.enter_context(nc.sbuf_tensor("x1t", [128, NCH_M * TB], F32))
        ln2p = _st.enter_context(nc.sbuf_tensor("ln2p", [128, TB], BF16))
        carry = _st.enter_context(nc.sbuf_tensor("carry", [128, NCH_M], BF16))
        ffntmp = _st.enter_context(nc.sbuf_tensor("ffntmp", [128, TB], BF16))
        x2t = _st.enter_context(nc.sbuf_tensor("x2t", [128, 2 * TB], F32))
        wslab = _st.enter_context(nc.sbuf_tensor("wslab", [128, 2 * 16 * 128], BF16))
        wsx = _st.enter_context(nc.sbuf_tensor("wsx", [128, NCH_I * 96], BF16))
        wsdt = _st.enter_context(nc.sbuf_tensor("wsdt", [64, D_INNER], BF16))
        cwc = _st.enter_context(nc.sbuf_tensor("cwc", [128, NCH_I * D_CONV], F32))
        cbc = _st.enter_context(nc.sbuf_tensor("cbc", [128, NCH_I], F32))
        bdtc = _st.enter_context(nc.sbuf_tensor("bdtc", [128, NCH_I], F32))
        dc = _st.enter_context(nc.sbuf_tensor("dc", [128, NCH_I], F32))
        murc = _st.enter_context(nc.sbuf_tensor("murc", [128, NCH_M], F32))
        mukc = _st.enter_context(nc.sbuf_tensor("mukc", [128, NCH_M], F32))
        omrc = _st.enter_context(nc.sbuf_tensor("omrc", [128, NCH_M], F32))
        omkc = _st.enter_context(nc.sbuf_tensor("omkc", [128, NCH_M], F32))
        arow = _st.enter_context(nc.sbuf_tensor("arow", [1, D_STATE], F32))
        abc = _st.enter_context(nc.sbuf_tensor("abc", [128, D_STATE], F32))
        eyeb = _st.enter_context(nc.sbuf_tensor("eyeb", [128, 128], BF16))
        eyef = _st.enter_context(nc.sbuf_tensor("eyef", [128, 128], F32))
        onesf = _st.enter_context(nc.sbuf_tensor("onesf", [128, 1], F32))
        ln2t = lnt
        xft = szt.ap()[:, 0:NCH_M * TB]
        xkt = szt.ap()[:, NCH_M * TB:2 * NCH_M * TB]
        rt = dtt.ap()[:, 0:NCH_M * TB]
        kat = dtt.ap()[:, NCH_M * TB:2 * NCH_M * TB]
        ps_a = _st.enter_context(nc.psum_tensor("ps_a", [128, 512], F32))
        ps_b = _st.enter_context(nc.psum_tensor("ps_b", [128, 512], F32))
        ps_c = _st.enter_context(nc.psum_tensor("ps_c", [128, 512], F32))
        ps_d = _st.enter_context(nc.psum_tensor("ps_d", [128, 512], F32))
        ps_y0 = _st.enter_context(nc.psum_tensor("ps_y0", [128, 512], F32))
        ps_y1 = _st.enter_context(nc.psum_tensor("ps_y1", [128, 512], F32))
        ps_s1 = _st.enter_context(nc.psum_tensor("ps_s1", [1, 512], F32))
        ps_s2 = _st.enter_context(nc.psum_tensor("ps_s2", [1, 512], F32))
        _q_w = _st.enter_context(nc.semaphore("q_w"))
        _q_x = _st.enter_context(nc.semaphore("q_x"))
        _q_g = _st.enter_context(nc.semaphore("q_g"))
        _q_o = _st.enter_context(nc.semaphore("q_o"))
        _s_T = _st.enter_context(nc.semaphore("s_T"))
        _s_V = _st.enter_context(nc.semaphore("s_V"))
        _s_S = _st.enter_context(nc.semaphore("s_S"))
        block = _st.enter_context(nc.Block())
        q_w, q_x, q_g, q_o = Sem(_q_w), Sem(_q_x), Sem(_q_g), Sem(_q_o)
        s_T, s_V, s_S = Sem(_s_T), Sem(_s_V), Sem(_s_S)
        class _Pending:
            __slots__ = ("incs",)
            def __init__(self):
                self.incs = []
            def then_inc(self, sem, n):
                self.incs.append((sem, n))
                return self

        class _Rec:
            def __init__(self):
                self.ops = []
            def __getattr__(self, meth):
                def call(*a, **kw):
                    p = _Pending()
                    self.ops.append((meth, a, kw, p))
                    return p
                return call
            def replay(self, eng):
                for meth, a, kw, p in self.ops:
                    instr = getattr(eng, meth)(*a, **kw)
                    for sem, n in p.incs:
                        instr.then_inc(sem, n)

        sy, g, t, v, s = _Rec(), _Rec(), _Rec(), _Rec(), _Rec()
        PS = [ps_a, ps_b, ps_c, ps_d]

        def dma(eng, q, **kw):
            eng.dma_start(**kw).then_inc(q.h, 16)
            q.v += 16

        def vi(instr):
            instr.then_inc(s_V.h, 1)
            s_V.v += 1

        def si(instr):
            instr.then_inc(s_S.h, 1)
            s_S.v += 1

        def ti(instr):
            instr.then_inc(s_T.h, 1)
            s_T.v += 1

        dma(sy, q_x, out=cwc[:], in_=conv_w[:])
        dma(sy, q_x, out=cbc[:], in_=conv_b[:])
        dma(sy, q_x, out=bdtc[:], in_=b_dt[:])
        dma(sy, q_x, out=dc[:], in_=D_in[:])
        dma(sy, q_x, out=murc[:], in_=mu_r[:])
        dma(sy, q_x, out=mukc[:], in_=mu_k[:])
        dma(sy, q_x, out=arow[:], in_=A_log[:])
        dma(sy, q_x, out=eyef[:], in_=eye_in[:])
        dma(g, q_g, out=wsdt[:], in_=W_dtT[:])
        for c in range(NCH_I):
            dma(g, q_g, out=wsx[:, c * 96:(c + 1) * 96],
                in_=W_xT[c * 128:(c + 1) * 128, :])

        s.wait_ge(q_x.h, q_x.v)
        si(s.activation(out=arow[:], in_=arow[:], func=AF.Exp))
        v.wait_ge(s_S.h, s_S.v)
        vi(v.tensor_scalar(arow[:], arow[:], -1.0, None, AluOpType.mult))
        g.wait_ge(s_V.h, s_V.v)
        dma(g, q_g, out=a_bounce[:], in_=arow[:])
        g.wait_ge(q_g.h, q_g.v)
        dma(g, q_g, out=abc[:], in_=a_bounce.broadcast_to((128, D_STATE)))

        v.wait_ge(q_x.h, q_x.v)
        vi(v.tensor_copy(eyeb[:], eyef[:]))
        vi(v.memset(onesf[:], 1.0))
        vi(v.memset(stc[:], 0.0))
        vi(v.memset(halo[:], 0.0))
        vi(v.memset(carry[:], 0.0))
        vi(v.tensor_scalar(omrc[:], murc[:], -1.0, 1.0, AluOpType.mult, AluOpType.add))
        vi(v.tensor_scalar(omkc[:], mukc[:], -1.0, 1.0, AluOpType.mult, AluOpType.add))

        def cast_weight(src, dst, R, C):
            for r0 in range(0, R, 128):
                sy.wait_ge(q_g.h, q_g.v)
                dma(sy, q_w, out=xt[:, :C], in_=src[r0:r0 + 128, :])
                v.wait_ge(q_w.h, q_w.v)
                vi(v.tensor_copy(gat[:, :C], xt[:, :C]))
                g.wait_ge(s_V.h, s_V.v)
                dma(g, q_g, out=dst[r0:r0 + 128, :], in_=gat[:, :C])

        cast_weight(W_inT, W_inT_bf, D_MODEL, 2 * D_INNER)
        cast_weight(W_outT, W_outT_bf, D_INNER, D_MODEL)
        cast_weight(W_rT, W_rT_bf, D_MODEL, D_MODEL)
        cast_weight(W_kT, W_kT_bf, D_MODEL, D_MODEL)
        cast_weight(W_vT, W_vT_bf, D_MODEL, D_MODEL)

        SLABW = 16 * 128
        slab_guard = [0, 0]

        def load_wslab(src_bf, R, m0, buf):
            nk = R // 128
            sy.wait_ge(s_T.h, slab_guard[buf])
            dma(sy, q_w,
                out=wslab[:, buf * SLABW: buf * SLABW + nk * 128]
                .rearrange("p (k c) -> p k c", c=128),
                in_=src_bf[:, m0 * 128:(m0 + 1) * 128]
                .rearrange("(k p) c -> p k c", p=128))

        def ln_stats_apply(src_t, dst_t, par, dump_t0=None):
            nch = NCH_M
            for c in range(nch):
                sl = src_t[:, c * TB:(c + 1) * TB]
                s.wait_ge(s_T.h, s_T.v)
                si(s.activation(out=xsq[:], in_=sl, func=AF.Square))
                t.wait_ge(s_S.h, s_S.v)
                ti(t.matmul(ps_s1[:, :], onesf[:], sl, start=(c == 0), stop=(c == nch - 1)))
                ti(t.matmul(ps_s2[:, :], onesf[:], xsq[:], start=(c == 0), stop=(c == nch - 1)))
            v.wait_ge(s_T.h, s_T.v)
            r_mean = rows[0:1, 0:TB]
            r_q = rows[0:1, TB:2 * TB]
            r_var = rows[0:1, 2 * TB:3 * TB]
            vi(v.tensor_scalar(r_mean, ps_s1[0:1, :], 1.0 / D_MODEL, None, AluOpType.mult))
            vi(v.tensor_scalar(r_q, ps_s2[0:1, :], 1.0 / D_MODEL, None, AluOpType.mult))
            vi(v.tensor_tensor(r_var, r_mean, r_mean, AluOpType.mult))
            vi(v.tensor_tensor(r_var, r_q, r_var, AluOpType.subtract))
            vi(v.tensor_scalar(r_var, r_var, EPS, None, AluOpType.add))
            s.wait_ge(s_V.h, s_V.v)
            si(s.activation(out=r_var, in_=r_var, func=AF.Ln))
            si(s.activation(out=r_var, in_=r_var, func=AF.Exp, scale=-0.5))
            g.wait_ge(s_V.h, s_V.v)
            g.wait_ge(s_S.h, s_S.v)
            dma(g, q_g, out=bounce[par][0:1, :], in_=r_mean)
            dma(g, q_g, out=bounce[par][1:2, :], in_=r_var)
            g.wait_ge(q_g.h, q_g.v)
            dma(g, q_g, out=mrep[:], in_=bounce[par][0:1, :].broadcast_to((128, TB)))
            dma(g, q_g, out=rrep[:], in_=bounce[par][1:2, :].broadcast_to((128, TB)))
            v.wait_ge(q_g.h, q_g.v)
            for c in range(nch):
                sl = src_t[:, c * TB:(c + 1) * TB]
                dl = dst_t[:, c * TB:(c + 1) * TB]
                vi(v.tensor_tensor(convacc[:, 0:TB], sl, mrep[:], AluOpType.subtract))
                vi(v.tensor_tensor(dl, convacc[:, 0:TB], rrep[:], AluOpType.mult))
            if debug and dump_t0 is not None:
                g.wait_ge(s_V.h, s_V.v)
                dma(g, q_g, out=dbg["d_mrep"][:, dump_t0:dump_t0 + TB], in_=mrep[:])
                dma(g, q_g, out=dbg["d_rrep"][:, dump_t0:dump_t0 + TB], in_=rrep[:])
                v.wait_ge(q_g.h, q_g.v)

        def xs_c(c):
            return xrt[:, c * (TB + 3) + 3:(c + 1) * (TB + 3)]

        for blk in range(NB):
            par = blk & 1
            t0 = blk * TB

            sy.wait_ge(s_V.h, s_V.v)
            for c in range(NCH_M):
                dma(sy, q_x, out=xt[:, c * TB:(c + 1) * TB],
                    in_=xT[c * 128:(c + 1) * 128, t0:t0 + TB])
            s.wait_ge(q_x.h, q_x.v)
            v.wait_ge(q_x.h, q_x.v)
            t.wait_ge(q_x.h, q_x.v)

            ln_stats_apply(xt, lnt, par, dump_t0=t0)

            qw_base = q_w.v
            ps_guard = {0: (0, 0), 1: (0, 0)}
            load_wslab(W_inT_bf, D_MODEL, 0, 0)
            t.wait_ge(s_V.h, s_V.v)
            for m in range(2 * NCH_I):
                buf = m & 1
                if m + 1 < 2 * NCH_I:
                    load_wslab(W_inT_bf, D_MODEL, m + 1, (m + 1) & 1)
                t.wait_ge(q_w.h, 16 * ((m + 1) * 1) + qw_base)
                gv, gs = ps_guard[m & 1]
                t.wait_ge(s_V.h, gv)
                t.wait_ge(s_S.h, gs)
                ps = PS[m & 1]
                for k in range(NCH_M):
                    ti(t.matmul(ps[:, :],
                                wslab[:, buf * SLABW + k * 128: buf * SLABW + (k + 1) * 128],
                                lnt[:, k * TB:(k + 1) * TB],
                                start=(k == 0), stop=(k == NCH_M - 1)))
                slab_guard[buf] = s_T.v
                if m < NCH_I:
                    v.wait_ge(s_T.h, s_T.v)
                    vi(v.tensor_copy(xrt[:, m * (TB + 3) + 3:(m + 1) * (TB + 3)], ps[:, :]))
                    ps_guard[m & 1] = (s_V.v, 0)
                else:
                    c = m - NCH_I
                    s.wait_ge(s_T.h, s_T.v)
                    si(s.activation(out=szt[:, c * TB:(c + 1) * TB], in_=ps[:, :], func=AF.Silu))
                    ps_guard[m & 1] = (ps_guard[m & 1][0], s_S.v)

            for c in range(NCH_I):
                xr_ch = xrt[:, c * (TB + 3):(c + 1) * (TB + 3)]
                v.wait_ge(s_S.h, s_S.v)
                vi(v.tensor_copy(xr_ch[:, 0:3], halo[:, c * 3:(c + 1) * 3]))
                acc0 = convacc[:, 0:TB]
                acc1 = convacc[:, TB:2 * TB]
                vi(v.tensor_scalar(acc0, xr_ch[:, 0:TB], cwc[:, c * 4:c * 4 + 1], None, AluOpType.mult))
                vi(v.scalar_tensor_tensor(acc1, xr_ch[:, 1:1 + TB], cwc[:, c * 4 + 1:c * 4 + 2], acc0,
                                          AluOpType.mult, AluOpType.add))
                vi(v.scalar_tensor_tensor(acc0, xr_ch[:, 2:2 + TB], cwc[:, c * 4 + 2:c * 4 + 3], acc1,
                                          AluOpType.mult, AluOpType.add))
                vi(v.scalar_tensor_tensor(acc1, xr_ch[:, 3:3 + TB], cwc[:, c * 4 + 3:c * 4 + 4], acc0,
                                          AluOpType.mult, AluOpType.add))
                vi(v.tensor_copy(halo[:, c * 3:(c + 1) * 3], xr_ch[:, TB:TB + 3]))
                s.wait_ge(s_V.h, s_V.v)
                si(s.activation(out=xr_ch[:, 3:3 + TB], in_=acc1, func=AF.Silu,
                                bias=cbc[:, c:c + 1]))

            t.wait_ge(s_S.h, s_S.v)
            t.wait_ge(s_V.h, s_V.v)
            for k in range(NCH_I):
                ti(t.matmul(ps_a[0:96, :], wsx[:, k * 96:(k + 1) * 96], xs_c(k),
                            start=(k == 0), stop=(k == NCH_I - 1)))
            v.wait_ge(s_T.h, s_T.v)
            vi(v.tensor_copy(xdbl[:, :], ps_a[0:96, :]))
            vi(v.tensor_copy(dtr[:, :], ps_a[0:64, :]))
            g.wait_ge(s_V.h, s_V.v)
            dma(g, q_g, out=bbc[par][:, :], in_=xdbl[64:96, :])
            g.wait_ge(q_g.h, q_g.v)
            for si_ in range(D_STATE):
                dma(g, q_g, out=brep[:, si_ * TB:(si_ + 1) * TB],
                    in_=bbc[par][si_:si_ + 1, :].broadcast_to((128, TB)))
                dma(g, q_g, out=crep[:, si_ * TB:(si_ + 1) * TB],
                    in_=bbc[par][16 + si_:17 + si_, :].broadcast_to((128, TB)))

            t.wait_ge(s_V.h, s_V.v)
            for m in range(NCH_I):
                ps = PS[2 + (m & 1)]
                t.wait_ge(s_S.h, s_S.v)
                ti(t.matmul(ps[:, :], wsdt[:, m * 128:(m + 1) * 128], dtr[:, :],
                            start=True, stop=True))
                s.wait_ge(s_T.h, s_T.v)
                si(s.activation(out=xsq[:], in_=ps[:, :], func=AF.Exp,
                                bias=bdtc[:, m:m + 1]))
                si(s.activation(out=dtt[:, m * TB:(m + 1) * TB], in_=xsq[:],
                                func=AF.Ln, bias=1.0))

            v.wait_ge(q_g.h, q_g.v)
            v.wait_ge(s_S.h, s_S.v)
            for c in range(NCH_I):
                dt_ch = dtt[:, c * TB:(c + 1) * TB]
                vi(v.tensor_tensor(vch[:], dt_ch, xs_c(c), AluOpType.mult))
                psy = [ps_y0, ps_y1][c & 1]
                for si_ in range(D_STATE):
                    ab = abuf[:, (si_ & 3) * TB:((si_ & 3) + 1) * TB]
                    hb = hbuf[:, (si_ & 1) * TB:((si_ & 1) + 1) * TB]
                    chb = chbuf[:, (si_ & 1) * TB:((si_ & 1) + 1) * TB]
                    s.wait_ge(s_V.h, max(0, s_V.v - 13))
                    si(s.activation(out=ab, in_=dt_ch, func=AF.Exp,
                                    scale=abc[:, si_:si_ + 1]))
                    v.wait_ge(s_S.h, s_S.v)
                    vi(v.tensor_tensor(ubuf[:], vch[:], brep[:, si_ * TB:(si_ + 1) * TB],
                                       AluOpType.mult))
                    stcol = stc[:, c * D_STATE + si_: c * D_STATE + si_ + 1]
                    v.wait_ge(s_T.h, max(0, s_T.v - 1))
                    scan_cnt = s_V.v + 1
                    vi(v.tensor_tensor_scan(hb, ab, ubuf[:], stcol,
                                            AluOpType.mult, AluOpType.add))
                    vi(v.tensor_tensor(chb, hb, crep[:, si_ * TB:(si_ + 1) * TB],
                                       AluOpType.mult))
                    v.wait_ge(s_V.h, scan_cnt)
                    vi(v.tensor_scalar(stcol, hb[:, TB - 1:TB], 1.0, None, AluOpType.mult))
                    t.wait_ge(s_V.h, s_V.v)
                    ti(t.matmul(psy[:, :], eyeb[:], chb,
                                start=(si_ == 0), stop=(si_ == D_STATE - 1)))
                v.wait_ge(s_T.h, s_T.v)
                vi(v.tensor_scalar(ubuf[:], xs_c(c), dc[:, c:c + 1], None, AluOpType.mult))
                vi(v.tensor_tensor(ytmp[:], psy[:, :], ubuf[:], AluOpType.add))
                vi(v.tensor_tensor(gat[:, c * TB:(c + 1) * TB], ytmp[:],
                                   szt[:, c * TB:(c + 1) * TB], AluOpType.mult))

            if debug:
                g.wait_ge(s_V.h, s_V.v)
                for c in range(NCH_M):
                    dma(g, q_g, out=dbg["d_ln"][c * 128:(c + 1) * 128, t0:t0 + TB],
                        in_=lnt[:, c * TB:(c + 1) * TB])
                for c in range(NCH_I):
                    dma(g, q_g, out=dbg["d_xs"][c * 128:(c + 1) * 128, t0:t0 + TB],
                        in_=xs_c(c))
                    dma(g, q_g, out=dbg["d_dt"][c * 128:(c + 1) * 128, t0:t0 + TB],
                        in_=dtt[:, c * TB:(c + 1) * TB])
                    dma(g, q_g, out=dbg["d_gy"][c * 128:(c + 1) * 128, t0:t0 + TB],
                        in_=gat[:, c * TB:(c + 1) * TB])

            if debug:
                v.wait_ge(q_g.h, q_g.v)
            qw_base = q_w.v
            ps_guard = {0: 0, 1: 0}
            load_wslab(W_outT_bf, D_INNER, 0, 0)
            t.wait_ge(s_V.h, s_V.v)
            for m in range(NCH_M):
                buf = m & 1
                if m + 1 < NCH_M:
                    load_wslab(W_outT_bf, D_INNER, m + 1, (m + 1) & 1)
                t.wait_ge(q_w.h, 16 * (m + 1) + qw_base)
                t.wait_ge(s_V.h, ps_guard[m & 1])
                ps = PS[m & 1]
                for k in range(NCH_I):
                    ti(t.matmul(ps[:, :],
                                wslab[:, buf * SLABW + k * 128: buf * SLABW + (k + 1) * 128],
                                gat[:, k * TB:(k + 1) * TB],
                                start=(k == 0), stop=(k == NCH_I - 1)))
                slab_guard[buf] = s_T.v
                v.wait_ge(s_T.h, s_T.v)
                vi(v.tensor_tensor(x1t[:, m * TB:(m + 1) * TB], ps[:, :],
                                   xt[:, m * TB:(m + 1) * TB], AluOpType.add))
                ps_guard[m & 1] = s_V.v

            if debug:
                g.wait_ge(s_V.h, s_V.v)
                for c in range(NCH_M):
                    dma(g, q_g, out=dbg["d_x1"][c * 128:(c + 1) * 128, t0:t0 + TB],
                        in_=x1t[:, c * TB:(c + 1) * TB])

            if debug:
                v.wait_ge(q_g.h, q_g.v)
            ln_stats_apply(x1t, ln2t, par)

            if debug:
                g.wait_ge(s_V.h, s_V.v)
                for c in range(NCH_M):
                    dma(g, q_g, out=dbg["d_ln2"][c * 128:(c + 1) * 128, t0:t0 + TB],
                        in_=ln2t[:, c * TB:(c + 1) * TB])

            if debug:
                v.wait_ge(q_g.h, q_g.v)
            for c in range(NCH_M):
                l2 = ln2t[:, c * TB:(c + 1) * TB]
                vi(v.tensor_copy(ln2p[:, 0:1], carry[:, c:c + 1]))
                vi(v.tensor_copy(ln2p[:, 1:TB], l2[:, 0:TB - 1]))
                vi(v.tensor_copy(carry[:, c:c + 1], l2[:, TB - 1:TB]))
                xf = xft[:, c * TB:(c + 1) * TB]
                xk = xkt[:, c * TB:(c + 1) * TB]
                vi(v.tensor_scalar(xf, l2, murc[:, c:c + 1], None, AluOpType.mult))
                vi(v.scalar_tensor_tensor(ffntmp[:], ln2p[:], omrc[:, c:c + 1], xf,
                                          AluOpType.mult, AluOpType.add))
                vi(v.tensor_copy(xf, ffntmp[:]))
                vi(v.tensor_scalar(xk, l2, mukc[:, c:c + 1], None, AluOpType.mult))
                vi(v.scalar_tensor_tensor(ffntmp[:], ln2p[:], omkc[:, c:c + 1], xk,
                                          AluOpType.mult, AluOpType.add))
                vi(v.tensor_copy(xk, ffntmp[:]))

            qw_base = q_w.v
            ps_guard = {0: 0, 1: 0}
            load_wslab(W_rT_bf, D_MODEL, 0, 0)
            t.wait_ge(s_V.h, s_V.v)
            t.wait_ge(s_S.h, s_S.v)
            for m in range(NCH_M):
                buf = m & 1
                if m + 1 < NCH_M:
                    load_wslab(W_rT_bf, D_MODEL, m + 1, (m + 1) & 1)
                t.wait_ge(q_w.h, 16 * (m + 1) + qw_base)
                t.wait_ge(s_S.h, ps_guard[m & 1])
                ps = PS[m & 1]
                for k in range(NCH_M):
                    ti(t.matmul(ps[:, :],
                                wslab[:, buf * SLABW + k * 128: buf * SLABW + (k + 1) * 128],
                                xft[:, k * TB:(k + 1) * TB],
                                start=(k == 0), stop=(k == NCH_M - 1)))
                slab_guard[buf] = s_T.v
                s.wait_ge(s_T.h, s_T.v)
                si(s.activation(out=rt[:, m * TB:(m + 1) * TB], in_=ps[:, :], func=AF.Sigmoid))
                ps_guard[m & 1] = s_S.v
            qw_base = q_w.v
            ps_guard = {0: 0, 1: 0}
            load_wslab(W_kT_bf, D_MODEL, 0, 0)
            for m in range(NCH_M):
                buf = m & 1
                if m + 1 < NCH_M:
                    load_wslab(W_kT_bf, D_MODEL, m + 1, (m + 1) & 1)
                t.wait_ge(q_w.h, 16 * (m + 1) + qw_base)
                t.wait_ge(s_S.h, ps_guard[m & 1])
                ps = PS[2 + (m & 1)]
                for k in range(NCH_M):
                    ti(t.matmul(ps[:, :],
                                wslab[:, buf * SLABW + k * 128: buf * SLABW + (k + 1) * 128],
                                xkt[:, k * TB:(k + 1) * TB],
                                start=(k == 0), stop=(k == NCH_M - 1)))
                slab_guard[buf] = s_T.v
                s.wait_ge(s_T.h, s_T.v)
                si(s.activation(out=xsq[:], in_=ps[:, :], func=AF.Relu))
                si(s.activation(out=kat[:, m * TB:(m + 1) * TB], in_=xsq[:], func=AF.Square))
                ps_guard[m & 1] = s_S.v

            qw_base = q_w.v
            ps_guard = {0: 0, 1: 0}
            load_wslab(W_vT_bf, D_MODEL, 0, 0)
            t.wait_ge(s_S.h, s_S.v)
            for m in range(NCH_M):
                buf = m & 1
                if m + 1 < NCH_M:
                    load_wslab(W_vT_bf, D_MODEL, m + 1, (m + 1) & 1)
                t.wait_ge(q_w.h, 16 * (m + 1) + qw_base)
                t.wait_ge(s_V.h, ps_guard[m & 1])
                ps = PS[m & 1]
                for k in range(NCH_M):
                    ti(t.matmul(ps[:, :],
                                wslab[:, buf * SLABW + k * 128: buf * SLABW + (k + 1) * 128],
                                kat[:, k * TB:(k + 1) * TB],
                                start=(k == 0), stop=(k == NCH_M - 1)))
                slab_guard[buf] = s_T.v
                v.wait_ge(s_T.h, s_T.v)
                v.wait_ge(q_o.h, max(0, q_o.v - 16))
                x2b = x2t[:, (m & 1) * TB:((m & 1) + 1) * TB]
                vi(v.tensor_tensor(ffntmp[:], ps[:, :], rt[:, m * TB:(m + 1) * TB],
                                   AluOpType.mult))
                vi(v.tensor_tensor(x2b, ffntmp[:], x1t[:, m * TB:(m + 1) * TB],
                                   AluOpType.add))
                ps_guard[m & 1] = s_V.v
                sy.wait_ge(s_V.h, s_V.v)
                dma(sy, q_o, out=out[m * 128:(m + 1) * 128, t0:t0 + TB], in_=x2b)

        if debug:
            g.wait_ge(s_V.h, s_V.v)
            dma(g, q_g, out=dbg["d_stc"][:, :], in_=stc[:, :])
        g.wait_ge(q_o.h, q_o.v)
        if debug:
            g.wait_ge(q_g.h, q_g.v)

        @block.sync
        def _(e):
            sy.replay(e)

        @block.gpsimd
        def _(e):
            g.replay(e)

        @block.tensor
        def _(e):
            t.replay(e)

        @block.vector
        def _(e):
            v.replay(e)

        @block.scalar
        def _(e):
            s.replay(e)

    return nc


_PROG_CACHE = {}


def _get_prog(T, TB, debug=False):
    key = (T, TB, debug)
    if key not in _PROG_CACHE:
        _PROG_CACHE[key] = build_program(T, TB, debug)
    return _PROG_CACHE[key]


def make_in_map(b, T, L0_attn_x, L0_attn_W_in, L0_attn_conv_w, L0_attn_conv_b,
                L0_attn_A_log, L0_attn_D, L0_attn_W_x, L0_attn_W_dt, L0_attn_b_dt,
                L0_attn_W_out, L0_ffn_W_r, L0_ffn_W_k, L0_ffn_W_v, L0_ffn_mu_r,
                L0_ffn_mu_k):
    C = np.ascontiguousarray
    f = np.float32
    return {
        "xT": C(np.asarray(L0_attn_x)[b, :T, :].T.astype(f)),
        "W_inT": C(np.asarray(L0_attn_W_in).T.astype(f)),
        "conv_wc": C(np.asarray(L0_attn_conv_w).reshape(NCH_I, 128, 4)
                     .transpose(1, 0, 2).reshape(128, NCH_I * 4).astype(f)),
        "conv_bc": C(np.asarray(L0_attn_conv_b).reshape(NCH_I, 128).T.astype(f)),
        "A_log_r": C(np.asarray(L0_attn_A_log).reshape(1, D_STATE).astype(f)),
        "D_c": C(np.asarray(L0_attn_D).reshape(NCH_I, 128).T.astype(f)),
        "W_xT": C(np.asarray(L0_attn_W_x).T.astype(f)),
        "W_dtT": C(np.asarray(L0_attn_W_dt).T.astype(f)),
        "b_dtc": C(np.asarray(L0_attn_b_dt).reshape(NCH_I, 128).T.astype(f)),
        "W_outT": C(np.asarray(L0_attn_W_out).T.astype(f)),
        "W_rT": C(np.asarray(L0_ffn_W_r).T.astype(f)),
        "W_kT": C(np.asarray(L0_ffn_W_k).T.astype(f)),
        "W_vT": C(np.asarray(L0_ffn_W_v).T.astype(f)),
        "mu_rc": C(np.asarray(L0_ffn_mu_r).reshape(NCH_M, 128).T.astype(f)),
        "mu_kc": C(np.asarray(L0_ffn_mu_k).reshape(NCH_M, 128).T.astype(f)),
        "eye_in": np.eye(128, dtype=f),
    }


def kernel(**inputs):
    T = 8192
    nc = _get_prog(T, 512)
    in_maps = [make_in_map(c >> 1, T, **inputs) for c in range(8)]
    res = run_bass_kernel_spmd(nc, in_maps, core_ids=list(range(8)))
    B = 4
    outs = []
    for b in range(B):
        o = res.results[2 * b]["out"]
        outs.append(np.ascontiguousarray(o.T)[None])
    return np.concatenate(outs, axis=0)
